# version 01
# baseline (speedup 1.0000x reference)
import math
import numpy as np
from contextlib import ExitStack

import concourse.bass as bass
import concourse.mybir as mybir
from concourse.ap import AP
from concourse.bass_utils import run_bass_kernel_spmd

F32 = mybir.dt.float32
BF16 = mybir.dt.bfloat16
AF = mybir.ActivationFunctionType
ALU = mybir.AluOpType
AXX = mybir.AxisListType.X

S = 8192
HID = 1024
H = 16
DH = 64
SOFTCAP = 50.0
Q_SCALE = DH ** (-0.5) / math.log(2)
K_SCALE = math.log(1.0 + math.e) / math.log(2)
EPS = 1e-5

NST = 8
DBG_NST = None
DBG_PARTS = 'all'
DBG_ATTN = 6
STT = 1024
NTILE = 8
NLN = 10
KV_LEN = 1152
HPN = 8


def ULIN(h):
    return (2 * (h // 8) + (h % 2)) * 4 + (h // 2) % 4


HINV = [None] * 16
for _h in range(16):
    HINV[ULIN(_h)] = _h


class Trk:

    DMAQ = ('g', 'sy', 'scd')
    OWNER = {'g': 'g', 'sy': 'sy', 'scd': 'sc'}

    def __init__(self, engines):
        self.engines = engines
        self.semnames = list(engines) + ['scd']
        self.cnt = {e: 0 for e in self.semnames}
        self.waited = {e: {s: 0 for s in self.semnames} for e in engines}
        self.idx = {e: 0 for e in engines}
        self.last_drain = {e: -1 for e in engines}
        self.q = {e: [] for e in engines}
        self.b = {}
        self.alias = {}
        self.bounds = {s: [0] for s in self.DMAQ}

    def _ent(self, key):
        if key not in self.b:
            self.b[key] = {'w': {}, 'r': {}, 'wi': {}}
        return self.b[key]

    def add_alias(self, *keys):
        ks = set(keys)
        for k in keys:
            self.alias.setdefault(k, set()).update(ks - {k})

    def _need(self, ename, mydma, deps):
        for sem, val in deps.items():
            if sem == ename and not mydma and sem not in self.DMAQ:
                continue
            if sem in self.DMAQ:
                val = self.cnt[sem]
            if val == 0:
                continue
            if self.waited[ename][sem] < val:
                self.q[ename].append(('wait', sem, val))
                self.waited[ename][sem] = val
                for ds, own in self.OWNER.items():
                    if own == ename:
                        self.bounds[ds].append(self.cnt[ds])

    def _maybe_drain(self, ename, keys):
        if ename not in ('v', 'sc'):
            return
        need = False
        for key in keys:
            wi = self._ent(key)['wi'].get(ename)
            if wi is not None and wi > self.last_drain[ename] and self.idx[ename] - wi < 4:
                need = True
        if need:
            self.q[ename].append(('drain',))
            self.last_drain[ename] = self.idx[ename]
            self.idx[ename] += 1

    def emit(self, ename, ins, outs, fn, dma=False):
        sem = ename
        if ename == 'sc' and dma:
            sem = 'scd'
        inc = 16 if dma else 1
        for k in ins:
            self._need(ename, dma, self._ent(k)['w'])
        self._maybe_drain(ename, ins)
        for k in outs:
            for kk in {k} | self.alias.get(k, set()):
                ent = self._ent(kk)
                self._need(ename, dma, ent['w'])
                self._need(ename, dma, ent['r'])
        self.q[ename].append(('op', fn, inc, sem))
        self.cnt[sem] += inc
        val = self.cnt[sem]
        if not dma:
            self.waited[ename][sem] = val
        myidx = self.idx[ename]
        self.idx[ename] += 1
        for k in ins:
            self._ent(k)['r'][sem] = val
        for k in outs:
            ent = self._ent(k)
            ent['w'] = {sem: val}
            ent['r'] = {}
            ent['wi'] = {ename: myidx} if not dma else {}

    def finalize(self):
        import bisect
        bnds = {s: sorted(set(self.bounds[s] + [self.cnt[s]])) for s in self.DMAQ}
        for ename in self.engines:
            qq = self.q[ename]
            for i, item in enumerate(qq):
                if item[0] == 'wait' and item[1] in self.DMAQ:
                    bl = bnds[item[1]]
                    j = bisect.bisect_left(bl, item[2])
                    qq[i] = ('wait', item[1], bl[j])

    def replay(self, ename, eng, sems):
        for item in self.q[ename]:
            if item[0] == 'wait':
                eng.wait_ge(sems[item[1]], item[2])
            elif item[0] == 'drain':
                eng.drain()
            else:
                item[1](eng).then_inc(sems[item[3]], item[2])


def build_program():
    nc = bass.Bass("TRN2", target_bir_lowering=False)

    x_d = nc.dram_tensor("x", [S, HID], F32, kind="ExternalInput").ap()
    wqT_d = nc.dram_tensor("wqT", [HID, HID], F32, kind="ExternalInput").ap()
    wkT_d = nc.dram_tensor("wkT", [HID, HID], F32, kind="ExternalInput").ap()
    wvT_d = nc.dram_tensor("wvT", [HID, HID], F32, kind="ExternalInput").ap()
    wrelT_d = nc.dram_tensor("wrelT", [HID, HID], F32, kind="ExternalInput").ap()
    wpostT_d = nc.dram_tensor("wpostT", [HID, HID], F32, kind="ExternalInput").ap()
    combT_d = nc.dram_tensor("combT", [HID, HID], F32, kind="ExternalInput").ap()
    gwT_d = nc.dram_tensor("gwT", [HID, 8], F32, kind="ExternalInput").ap()
    pds_d = nc.dram_tensor("pds", [DH, 1], F32, kind="ExternalInput").ap()
    posT_d = nc.dram_tensor("posT", [HID, 71], F32, kind="ExternalInput").ap()
    expT_d = nc.dram_tensor("expT", [8, HID, HID], F32, kind="ExternalInput").ap()
    expb_d = nc.dram_tensor("expb", [8, HID], F32, kind="ExternalInput").ap()
    y_d = nc.dram_tensor("y", [S, HID], F32, kind="ExternalOutput").ap()
    vscr_d = nc.dram_tensor("vscr", [2 * H * 128 * 104], BF16, kind="Internal").ap()

    with ExitStack() as ctx:
        SB = lambda nm, p, f, dt=F32: ctx.enter_context(nc.sbuf_tensor(nm, [p, f], dt))

        reg1 = SB("reg1", 128, 32768, BF16)
        reg2 = SB("reg2", 128, 26624, BF16)
        xn = SB("xn", 128, 8 * 1280, BF16)
        partial = SB("partial", 128, 8 * 1024, BF16)
        scb = SB("scb", 128, H * 72, BF16)
        tt = SB("tt", 128, H * 72)
        ee = SB("ee", 128, H * 72)
        d2s = SB("d2s", 128, H * 104, BF16)
        bdt = SB("bdt", 128, H * 72, BF16)
        een = SB("een", 128, H * 192 + 96, BF16)
        egt = SB("egt", 128, 8 * 256, BF16)
        ega = [SB(f"ega{i}", 128, 128, BF16) for i in range(4)]
        egb = [SB(f"egb{i}", 128, 128, BF16) for i in range(4)]
        aot = SB("aot", 128, 8 * 128, BF16)
        ssum = SB("ssum", 128, H)
        rssum = SB("rssum", 128, H)
        xf = [SB(f"xf{i}", 128, 1024) for i in range(2)]
        stg = [SB(f"stg{i}", 128, 1024, BF16) for i in range(2)]
        mx = [SB(f"mx{i}", 128, 1024) for i in range(2)]
        glog = SB("glog", 128, 8)
        w8 = SB("w8", 128, 64)
        gA = SB("gA", 128, 1)
        gB = SB("gB", 128, 1)
        gC = SB("gC", 128, 8)
        gD = SB("gD", 128, 8)
        qsA = SB("qsA", 128, 1)
        epst = SB("epst", 128, 1)
        onesb = SB("onesb", 1, 128, BF16)
        rrv = SB("rrv", 128, HPN * 104, BF16)
        lns = SB("lns", 128, 8)
        pdsb = SB("pdsb", DH, 1)
        gwa = SB("gwa", 128, 64, BF16)

        psb = [ctx.enter_context(nc.psum_tensor(f"psb{i}", [128, 512], F32)) for i in range(8)]

        sems = {}
        for snm in ('g', 'sy', 't', 'v', 'sc', 'scd'):
            sems[snm] = ctx.enter_context(nc.semaphore("sem_" + snm))
        blk = ctx.enter_context(nc.Block())

        T = Trk(['g', 'sy', 't', 'v', 'sc'])

        REG1KEYS = ('r1slot0', 'r1slot1', 'wq', 'wk', 'wv', 'wpost', 'startup', 'posb', 'relk')
        T.add_alias(*REG1KEYS)
        REG2KEYS = ('karena', 'varena', 'qarena', 'combA', 'seA0', 'seA1', 'mixbf', 'brow')
        T.add_alias(*REG2KEYS)

        def r1(a, b):
            return reg1[:, a:b]

        def wsrc(dram_t, base_el=0):
            return AP(dram_t.tensor, base_el, [[1024, 128], [128 * 1024, 8], [1, 1024]])

        def wdst(reg, off):
            fs = reg.shape[1]
            return AP(reg[:, :].tensor, off, [[fs, 128], [1024, 8], [1, 1024]])

        def r2(a, b):
            return reg2[:, a:b]

        T.emit('g', [], ['pdsb'], lambda g: g.dma_start(out=pdsb[:, :], in_=pds_d[:, :]), dma=True)
        T.emit('sc', ['pdsb'], ['qs_e'],
               lambda s: s.activation(out=qsA[0:DH, :], in_=pdsb[:, :], func=AF.Exp))
        T.emit('v', ['qs_e'], ['qs_e1'],
               lambda v: v.tensor_scalar(qsA[0:DH, :], qsA[0:DH, :], 1.0, None, ALU.add))
        T.emit('sc', ['qs_e1'], ['qs_ln'],
               lambda s: s.activation(out=qsA[0:DH, :], in_=qsA[0:DH, :], func=AF.Ln))
        T.emit('v', ['qs_ln'], ['qs_tmp'],
               lambda v: v.tensor_scalar(qsA[0:DH, :], qsA[0:DH, :], float(Q_SCALE), None, ALU.mult))
        T.emit('sy', ['qs_tmp'], ['qsA'],
               lambda sy: sy.dma_start(out=qsA[DH:128, :], in_=qsA[0:DH, :]), dma=True)
        T.emit('v', [], ['epst'], lambda v: v.memset(epst[:, :], EPS))
        T.emit('v', [], ['onesb'], lambda v: v.memset(onesb[:, :], 1.0))
        for ich in range(8):
            T.emit('g', [], ['gwa'],
                   lambda g, ich=ich: g.dma_start(out=gwa[:, ich * 8:ich * 8 + 8],
                                                  in_=gwT_d[ich * 128:(ich + 1) * 128, :]),
                   dma=True)
        for t_, nm in ((een, 'een_z'), (egt, 'egt_z'), (scb, 'scb_z'), (tt, 'tt_z'),
                       (ee, 'ee_z'), (bdt, 'bdt_z'), (d2s, 'd2s_z'), (aot, 'aot_z')):
            T.emit('v', [], [nm], lambda v, t_=t_: v.memset(t_[:, :], 0.0))
        T.add_alias('een_z', *[f'een_{h}' for h in range(H)])
        T.add_alias('egt_z', 'mixfm', *[f'egt{i}' for i in range(8)])
        T.add_alias('scb_z', 'scb')
        T.add_alias('tt_z', 'tt')
        T.add_alias('ee_z', *[f'ee_{h}' for h in range(H)])
        T.add_alias('bdt_z', 'bdt')
        T.add_alias('d2s_z', *[f'd2s_{h}' for h in range(H)])
        T.add_alias('aot_z', 'aot')
        for j in range(NLN):
            T.add_alias('xn_zz', f'xn{j}')

        T.emit('g', [], ['startup'],
               lambda g: g.dma_start(out=wdst(reg1, 0), in_=wsrc(wrelT_d)), dma=True)
        for ich in range(8):
            T.emit('g', [], ['posb'],
                   lambda g, ich=ich: g.dma_start(out=r1(8192 + ich * 71, 8192 + ich * 71 + 71),
                                                  in_=posT_d[ich * 128:(ich + 1) * 128, :]),
                   dma=True)
        for och in range(8):
            pk = f'ps{och % 2}'
            for ich in range(8):
                T.emit('t', ['startup', 'posb'], [pk],
                       lambda t_, och=och, ich=ich: t_.matmul(
                           psb[och % 2][:, 0:71],
                           r1(ich * 1024 + och * 128, ich * 1024 + och * 128 + 128),
                           r1(8192 + ich * 71, 8192 + ich * 71 + 71),
                           start=(ich == 0), stop=(ich == 7)))
            T.emit('sc', [pk], ['relk'],
                   lambda s, och=och: s.activation(out=r1(16384 + och * 71, 16384 + och * 71 + 71),
                                                   in_=psb[och % 2][:, 0:71], func=AF.Copy))
        T.emit('v', [], ['rrv'], lambda v: v.memset(rrv[:, :], 0.0))
        for h in range(H):
            hp, b = h // 2, h % 2
            src0 = 16384 + hp * 71
            lo, hi = 64 * b, 64 * b + 64
            T.emit('v', ['relk', 'rrv'], ['rrv'],
                   lambda v, hp=hp, lo=lo, hi=hi, src0=src0:
                   v.tensor_copy(rrv[lo:hi, hp * 104:hp * 104 + 30],
                                 r1(src0 + 41, src0 + 71)[lo:hi, :]))
            T.emit('v', ['relk', 'rrv'], ['rrv'],
                   lambda v, hp=hp, lo=lo, hi=hi, src0=src0:
                   v.tensor_copy(rrv[lo:hi, hp * 104 + 31:hp * 104 + 102],
                                 r1(src0, src0 + 71)[lo:hi, :]))

        for st in range(DBG_NST if DBG_NST is not None else NST):
            t0 = st * STT
            kv0 = t0 - 31

            for j in range(NLN):
                ts = t0 + 128 * (j - 1)
                xnkey = f'xn{j}'
                if ts < 0 or ts >= S:
                    for ich in range(8):
                        T.emit('v', [], [xnkey],
                               lambda v, j=j, ich=ich: v.memset(
                                   xn[:, ich * 1280 + 128 * j: ich * 1280 + 128 * j + 128], 0.0))
                    continue
                xk = f'xf{j % 2}'
                T.emit('g', [], [xk],
                       lambda g, ts=ts, j=j: g.dma_start(out=xf[j % 2][:, :],
                                                         in_=x_d[ts:ts + 128, :]), dma=True)
                T.emit('v', [xk], ['lns_sx'],
                       lambda v, j=j: v.tensor_reduce(lns[:, 0:1], xf[j % 2][:, :], AXX, ALU.add))
                T.emit('v', [xk, 'ps6'], ['ps6', 'lns_sq0'],
                       lambda v, j=j: v.scalar_tensor_tensor(psb[6][:, :], xf[j % 2][:, 0:512], 1.0,
                                                             xf[j % 2][:, 0:512], ALU.mult, ALU.mult,
                                                             accum_out=lns[:, 1:2]))
                T.emit('v', [xk, 'ps6'], ['ps6', 'lns_sq1'],
                       lambda v, j=j: v.scalar_tensor_tensor(psb[6][:, :], xf[j % 2][:, 512:1024], 1.0,
                                                             xf[j % 2][:, 512:1024], ALU.mult, ALU.mult,
                                                             accum_out=lns[:, 7:8]))
                T.emit('v', ['lns_sq0', 'lns_sq1'], ['lns_sq'],
                       lambda v: v.tensor_add(lns[:, 1:2], lns[:, 1:2], lns[:, 7:8]))
                T.emit('v', ['lns_sx'], ['lns_negm'],
                       lambda v: v.tensor_scalar(lns[:, 2:3], lns[:, 0:1], -1.0 / HID, None, ALU.mult))
                T.emit('v', ['lns_negm'], ['lns_msq'],
                       lambda v: v.tensor_mul(lns[:, 3:4], lns[:, 2:3], lns[:, 2:3]))
                T.emit('v', ['lns_sq', 'lns_msq'], ['lns_var'],
                       lambda v: v.scalar_tensor_tensor(lns[:, 4:5], lns[:, 1:2], 1.0 / HID,
                                                        lns[:, 3:4], ALU.mult, ALU.subtract))
                T.emit('sc', ['lns_var', 'epst'], ['lns_sd'],
                       lambda s: s.activation(out=lns[:, 5:6], in_=lns[:, 4:5], func=AF.Sqrt,
                                              bias=epst[:, :]))
                T.emit('v', ['lns_sd'], ['lns_rsd'],
                       lambda v: v.reciprocal(lns[:, 6:7], lns[:, 5:6]))
                sk = f'stg{j % 2}'
                T.emit('v', [xk, 'lns_negm', 'lns_rsd'], [sk],
                       lambda v, j=j: v.tensor_scalar(stg[j % 2][:, :], xf[j % 2][:, :],
                                                      lns[:, 2:3], lns[:, 6:7], ALU.add, ALU.mult))
                for ich in range(8):
                    T.emit('sy', [sk], [xnkey],
                           lambda sy, j=j, ich=ich: sy.dma_start(
                               out=xn[:, ich * 1280 + 128 * j: ich * 1280 + 128 * j + 128],
                               in_=stg[j % 2][:, ich * 128:ich * 128 + 128], transpose=True),
                           dma=True)

            xnkeys = [f'xn{j}' for j in range(NLN)]
            core_xn = [f'xn{j}' for j in range(1, 9)]

            if DBG_PARTS == 'ln':
                for k in range(NTILE):
                    xk2 = f'xf{k % 2}'
                    T.emit('g', [], [xk2],
                           lambda g, k=k, t0=t0: g.dma_start(
                               out=xf[k % 2][:, :], in_=x_d[t0 + 128 * k:t0 + 128 * k + 128, :]),
                           dma=True)
                    T.emit('g', [xk2], [f'out_{st}_{k}'],
                           lambda g, k=k, t0=t0: g.dma_start(
                               out=y_d[t0 + 128 * k:t0 + 128 * k + 128, :],
                               in_=xf[k % 2][:, :]),
                           dma=True)
                continue
            T.emit('g', [], ['combA'],
                   lambda g: g.dma_start(out=wdst(reg2, 0), in_=wsrc(combT_d)), dma=True)
            T.emit('g', [], ['brow'],
                   lambda g: g.dma_start(out=reg2[0:1, 11264:11264 + 8192],
                                         in_=AP(expb_d.tensor, 0, [[8192, 1], [1, 8192]])),
                   dma=True)

            for k in range(NTILE):
                xnk = f'xn{k + 1}'
                for ich in range(8):
                    T.emit('t', [xnk, 'gwa'], ['ps7'],
                           lambda t_, k=k, ich=ich: t_.matmul(
                               psb[7][:, 0:8],
                               xn[:, ich * 1280 + 128 * (k + 1): ich * 1280 + 128 * (k + 1) + 128],
                               gwa[:, ich * 8:ich * 8 + 8],
                               start=(ich == 0), stop=(ich == 7)))
                T.emit('sc', ['ps7'], ['glog'],
                       lambda s, k=k: s.activation(out=glog[:, :], in_=psb[7][:, 0:8], func=AF.Exp))
                T.emit('v', ['glog'], ['gA'],
                       lambda v: v.tensor_reduce(gA[:, :], glog[:, :], AXX, ALU.add))
                T.emit('v', ['gA'], ['gB'], lambda v: v.reciprocal(gB[:, :], gA[:, :]))
                T.emit('v', ['glog', 'gB'], ['gC'],
                       lambda v: v.tensor_scalar(gC[:, :], glog[:, :], gB[:, :], None, ALU.mult))
                T.emit('v', ['gC'], ['gA2'],
                       lambda v: v.tensor_reduce(gA[:, :], gC[:, :], AXX, ALU.max))
                T.emit('v', ['gC', 'gA2'], ['gD'],
                       lambda v: v.tensor_scalar(gD[:, :], gC[:, :], gA[:, :], None, ALU.is_ge))
                T.emit('v', ['gD', 'gC'], ['gD2'],
                       lambda v: v.scalar_tensor_tensor(gD[:, :], gD[:, :], -1e9, gC[:, :],
                                                        ALU.mult, ALU.add))
                T.emit('v', ['gD2'], ['gB2'],
                       lambda v: v.tensor_reduce(gB[:, :], gD[:, :], AXX, ALU.max))
                T.emit('v', ['gC', 'gB2'], ['gD3'],
                       lambda v: v.tensor_scalar(gD[:, :], gC[:, :], gB[:, :], None, ALU.is_ge))
                T.emit('v', ['gC', 'gD3'], [f'w8_{k}'],
                       lambda v, k=k: v.tensor_mul(w8[:, 8 * k:8 * k + 8], gC[:, :], gD[:, :]))

            for gidx in range(4):
                slot = gidx % 2
                sk = f'r1slot{slot}'
                base = slot * 16384
                for e2 in range(2):
                    e = 2 * gidx + e2
                    T.emit('g', [], [sk],
                           lambda g, e=e, base=base, e2=e2: g.dma_start(
                               out=wdst(reg1, base + e2 * 8192),
                               in_=wsrc(expT_d, e * HID * HID)),
                           dma=True)
                for k in range(NTILE):
                    xnk = f'xn{k + 1}'
                    for e2 in range(2):
                        e = 2 * gidx + e2
                        for hf in range(2):
                            T.emit('t', ['brow', 'onesb'], [f'ps{2 * e2 + hf}'],
                                   lambda t_, e=e, e2=e2, hf=hf: t_.matmul(
                                       psb[2 * e2 + hf][:, :],
                                       onesb[:, :],
                                       reg2[0:1, 11264 + e * 1024 + hf * 512:
                                            11264 + e * 1024 + hf * 512 + 512],
                                       start=True, stop=False))
                    for ich in range(8):
                        for e2 in range(2):
                            for hf in range(2):
                                T.emit('t', [xnk, sk], [f'ps{2 * e2 + hf}'],
                                       lambda t_, k=k, ich=ich, e2=e2, hf=hf, base=base: t_.matmul(
                                           psb[2 * e2 + hf][:, :],
                                           xn[:, ich * 1280 + 128 * (k + 1):
                                              ich * 1280 + 128 * (k + 1) + 128],
                                           r1(base + e2 * 8192 + ich * 1024 + hf * 512,
                                              base + e2 * 8192 + ich * 1024 + hf * 512 + 512),
                                           start=False, stop=(ich == 7)))
                    for e2 in range(2):
                        e = 2 * gidx + e2
                        sek = f'seA{e2}'
                        for hf in range(2):
                            T.emit('sc', [f'ps{2 * e2 + hf}'], [sek],
                                   lambda s, e2=e2, hf=hf: s.activation(
                                       out=r2(8192 + e2 * 1024 + hf * 512,
                                              8192 + e2 * 1024 + hf * 512 + 512),
                                       in_=psb[2 * e2 + hf][:, :], func=AF.Silu))
                        if e == 0:
                            T.emit('v', [sek, f'w8_{k}'], [f'partial{k}'],
                                   lambda v, e2=e2, k=k: v.tensor_scalar(
                                       partial[:, k * 1024:k * 1024 + 1024],
                                       r2(8192 + e2 * 1024, 8192 + e2 * 1024 + 1024),
                                       w8[:, 8 * k:8 * k + 1], None, ALU.mult))
                        else:
                            T.emit('v', [sek, f'w8_{k}', f'partial{k}'], [f'partial{k}'],
                                   lambda v, e=e, e2=e2, k=k: v.scalar_tensor_tensor(
                                       partial[:, k * 1024:k * 1024 + 1024],
                                       r2(8192 + e2 * 1024, 8192 + e2 * 1024 + 1024),
                                       w8[:, 8 * k + e:8 * k + e + 1],
                                       partial[:, k * 1024:k * 1024 + 1024],
                                       ALU.mult, ALU.add))
                    if gidx == 3:
                        for ich in range(8):
                            T.emit('sy', [f'partial{k}'], ['mixfm'],
                                   lambda sy, ich=ich, k=k: sy.dma_start(
                                       out=egt[:, ich * 128:ich * 128 + 128],
                                       in_=partial[:, k * 1024 + ich * 128:
                                                   k * 1024 + ich * 128 + 128],
                                       transpose=True),
                                   dma=True)
                        for ich in range(8):
                            for hf in range(2):
                                T.emit('t', ['mixfm', 'combA'], [f'ps{4 + hf}'],
                                       lambda t_, ich=ich, hf=hf: t_.matmul(
                                           psb[4 + hf][:, :],
                                           egt[:, ich * 128:ich * 128 + 128],
                                           r2(ich * 1024 + hf * 512, ich * 1024 + hf * 512 + 512),
                                           start=(ich == 0), stop=(ich == 7)))
                        xk = f'xf{k % 2}'
                        T.emit('g', [], [xk],
                               lambda g, k=k, t0=t0: g.dma_start(
                                   out=xf[k % 2][:, :],
                                   in_=x_d[t0 + 128 * k:t0 + 128 * k + 128, :]),
                               dma=True)
                        for hf in range(2):
                            T.emit('v', [f'ps{4 + hf}', xk, f'partial{k}'], [f'partial{k}'],
                                   lambda v, k=k, hf=hf: v.tensor_tensor(
                                       partial[:, k * 1024 + hf * 512:k * 1024 + hf * 512 + 512],
                                       psb[4 + hf][:, :],
                                       xf[k % 2][:, hf * 512:hf * 512 + 512], ALU.add))
            T.emit('v', [], ['egt_rz'], lambda v: v.memset(egt[:, :], 0.0))
            T.add_alias('egt_rz', 'mixfm', *[f'egt{i}' for i in range(8)])

            if DBG_PARTS == 'ffn':
                for k in range(NTILE):
                    T.emit('g', [f'partial{k}'], [f'out_{st}_{k}'],
                           lambda g, k=k, t0=t0: g.dma_start(
                               out=y_d[t0 + 128 * k:t0 + 128 * k + 128, :],
                               in_=partial[:, k * 1024:k * 1024 + 1024]),
                           dma=True)
                continue
            for nm, dram in (('wq', wqT_d), ('wk', wkT_d), ('wv', wvT_d), ('wpost', wpostT_d)):
                off = {'wq': 0, 'wk': 8192, 'wv': 16384, 'wpost': 24576}[nm]
                T.emit('g', [], [nm],
                       lambda g, dram=dram, off=off: g.dma_start(
                           out=wdst(reg1, off), in_=wsrc(dram)), dma=True)

            for och in range(8):
                for ich in range(8):
                    for bb in range(2):
                        T.emit('t', ['wq'] + (core_xn if ich == 0 else []),
                               [f'ps{(och % 2) * 2 + bb}'],
                               lambda t_, och=och, ich=ich, bb=bb: t_.matmul(
                                   psb[(och % 2) * 2 + bb][:, :],
                                   r1(ich * 1024 + och * 128, ich * 1024 + och * 128 + 128),
                                   xn[:, ich * 1280 + 128 + bb * 512:
                                      ich * 1280 + 128 + bb * 512 + 512],
                                   start=(ich == 0), stop=(ich == 7)))
                for bb in range(2):
                    T.emit('sc', [f'ps{(och % 2) * 2 + bb}', 'qsA', 'qs_tmp'], ['qarena'],
                           lambda s, och=och, bb=bb: s.activation(
                               out=r2(18432 + och * 1024 + bb * 512,
                                      18432 + och * 1024 + bb * 512 + 512),
                               in_=psb[(och % 2) * 2 + bb][:, :],
                               func=AF.Copy, scale=qsA[:, :]))
            kblocks = [(0, 512), (512, 512), (1024, 128)]
            for och in range(8):
                for ich in range(8):
                    for bi, (bo, bn) in enumerate(kblocks):
                        T.emit('t', ['wk'] + (xnkeys if ich == 0 else []),
                               [f'ps{(och % 2) * 3 + bi}'],
                               lambda t_, och=och, ich=ich, bo=bo, bn=bn, bi=bi: t_.matmul(
                                   psb[(och % 2) * 3 + bi][:, 0:bn],
                                   r1(8192 + ich * 1024 + och * 128,
                                      8192 + ich * 1024 + och * 128 + 128),
                                   xn[:, ich * 1280 + 97 + bo: ich * 1280 + 97 + bo + bn],
                                   start=(ich == 0), stop=(ich == 7)))
                for bi, (bo, bn) in enumerate(kblocks):
                    T.emit('sc', [f'ps{(och % 2) * 3 + bi}'], ['karena'],
                           lambda s, och=och, bo=bo, bn=bn, bi=bi: s.activation(
                               out=r2(och * 1152 + bo, och * 1152 + bo + bn),
                               in_=psb[(och % 2) * 3 + bi][:, 0:bn],
                               func=AF.Copy, scale=float(K_SCALE)))
            for j in range(NLN):
                ts = t0 + 128 * (j - 1)
                if ts + 128 <= kv0 or ts >= kv0 + KV_LEN:
                    continue
                xnk = f'xn{j}'
                for hf in range(2):
                    for ich in range(8):
                        T.emit('t', [xnk, 'wv'], [f'ps{6 + hf}'],
                               lambda t_, j=j, ich=ich, hf=hf: t_.matmul(
                                   psb[6 + hf][:, :],
                                   xn[:, ich * 1280 + 128 * j: ich * 1280 + 128 * j + 128],
                                   r1(16384 + ich * 1024 + hf * 512,
                                      16384 + ich * 1024 + hf * 512 + 512),
                                   start=(ich == 0), stop=(ich == 7)))
                    T.emit('sc', [f'ps{6 + hf}'], [f'stg{j % 2}'],
                           lambda s, j=j, hf=hf: s.activation(
                               out=stg[j % 2][:, hf * 512:hf * 512 + 512],
                               in_=psb[6 + hf][:, :], func=AF.Copy))
                p0 = ts - kv0
                lo = max(0, -p0)
                hi = min(128, KV_LEN - p0)
                pa = p0 + lo
                while lo < hi:
                    blkn = pa // 128
                    row0 = pa % 128
                    n = min(hi - lo, 128 - row0)
                    T.emit('sy', [f'stg{j % 2}'], ['varena'],
                           lambda sy, lo=lo, row0=row0, blkn=blkn, n=n, j=j: sy.dma_start(
                               out=r2(9216 + blkn * 1024, 9216 + blkn * 1024 + 1024)[row0:row0 + n, :],
                               in_=stg[j % 2][lo:lo + n, :]),
                           dma=True)
                    lo += n
                    pa += n

            for k in range(NTILE if DBG_ATTN >= 2 else 0):
                for h in range(H):
                    hp, b = h // 2, h % 2
                    qcol = 18432 + hp * 1024 + 128 * k
                    pk = f'ps{4 + h % 2}'
                    T.emit('t', ['qarena', 'rrv'], [pk],
                           lambda t_, b=b, qcol=qcol, hp=hp, h=h: t_.matmul(
                               psb[4 + h % 2][:, 0:104],
                               r2(qcol, qcol + 128)[64 * b:64 * b + 64, :],
                               rrv[64 * b:64 * b + 64, hp * 104:hp * 104 + 104],
                               start=True, stop=True, tile_position=(64 * b, 0)))
                    T.emit('sc', [pk], [f'd2s_{h}'],
                           lambda s, h=h, u=ULIN(h): s.activation(
                               out=d2s[:, u * 104:u * 104 + 104],
                               in_=psb[4 + h % 2][:, 0:104], func=AF.Copy))
                ring = k % 2
                vbase = ring * H * 128 * 104
                d2keys = [f'd2s_{h}' for h in range(H)]
                T.emit('sc', d2keys, [f'vscr{ring}'],
                       lambda s, vbase=vbase: s.dma_start(
                           out=AP(vscr_d.tensor, vbase + 30, [[104, 128], [13312, H], [1, 74]]),
                           in_=AP(d2s[:, :].tensor, 30, [[H * 104, 128], [104, H], [1, 74]])),
                       dma=True)
                T.emit('sc', d2keys, [f'vscr{ring}'],
                       lambda s, vbase=vbase: s.dma_start(
                           out=AP(vscr_d.tensor, vbase + 104, [[104, 127], [13312, H], [1, 30]]),
                           in_=AP(d2s[:, :].tensor, 0, [[H * 104, 127], [104, H], [1, 30]])),
                       dma=True)
                for m in range(4):
                    T.emit('sc', [f'vscr{ring}'], ['bdt'],
                           lambda s, m=m, vbase=vbase: s.dma_start(
                               out=AP(bdt[:, :].tensor, 32 * m * (H * 72),
                                      [[H * 72, 32], [72, H], [1, 71]]),
                               in_=AP(vscr_d.tensor, vbase + m * 32 * 104 + 31,
                                      [[103, 32], [13312, H], [1, 71]])),
                           dma=True)
                if DBG_ATTN < 3:
                    continue
                for h in range(H):
                    hp, b = h // 2, h % 2
                    u = ULIN(h)
                    bank = u // 4
                    slot = u % 4
                    for m in range(4):
                        qcol = 18432 + hp * 1024 + 128 * k + 32 * m
                        kcol = hp * 1152 + 128 * k + 32 * m
                        T.emit('t', ['qarena', 'karena'], [f'ps{bank}'],
                               lambda t_, b=b, m=m, qcol=qcol, kcol=kcol, bank=bank, slot=slot:
                               t_.matmul(
                                   psb[bank][32 * m:32 * m + 32, slot * 72:slot * 72 + 72],
                                   r2(qcol, qcol + 32)[64 * b:64 * b + 64, :],
                                   r2(kcol, kcol + 72)[64 * b:64 * b + 64, :],
                                   start=True, stop=True, tile_position=(64 * b, 32 * m)))
                for bank in range(4 if DBG_ATTN >= 3.2 else 0):
                    T.emit('v', [f'ps{bank}', 'bdt'], ['scb'],
                           lambda v, bank=bank: v.tensor_tensor(
                               scb[:, bank * 288:bank * 288 + 288],
                               psb[bank][:, 0:288],
                               bdt[:, bank * 288:bank * 288 + 288], ALU.add))
                if DBG_ATTN >= 3.4:
                    T.emit('sc', ['scb'], ['tt'],
                           lambda s: s.activation(out=tt[:, :], in_=scb[:, :], func=AF.Tanh,
                                                  scale=1.0 / SOFTCAP))
                for h in range(H if DBG_ATTN >= 3.6 else 0):
                    T.emit('sc', ['tt'], [f'ee_{h}', f'ssum_{h}'],
                           lambda s, h=h: s.activation(out=ee[:, h * 72:h * 72 + 71],
                                                       in_=tt[:, h * 72:h * 72 + 71],
                                                       func=AF.Exp, scale=float(SOFTCAP),
                                                       accum_out=ssum[:, h:h + 1]))
                if DBG_ATTN >= 3.8:
                    T.emit('v', [f'ssum_{u}' for u in range(H)], ['rssum'],
                           lambda v: v.reciprocal(rssum[:, :], ssum[:, :]))
                for h in range(H if DBG_ATTN >= 3.8 else 0):
                    T.emit('v', [f'ee_{h}', 'rssum'], [f'een_{h}'],
                           lambda v, h=h: v.tensor_scalar(
                               een[:, 96 + 192 * h:96 + 192 * h + 71],
                               ee[:, h * 72:h * 72 + 71],
                               rssum[:, h:h + 1], None, ALU.mult))
                if DBG_ATTN < 4:
                    continue
                for grp in range(4):
                    for u in range(4 * grp, 4 * grp + 4):
                        eslot = u % 8
                        for m in range(4):
                            T.emit('sy', [f'een_{u}'], [f'egt{eslot}'],
                                   lambda sy, u=u, m=m, eslot=eslot: sy.dma_start(
                                       out=egt[32 * m:32 * m + 32, eslot * 256:eslot * 256 + 192],
                                       in_=een[32 * m:32 * m + 32,
                                               96 + 192 * u - 32 * m:96 + 192 * u - 32 * m + 192]),
                                   dma=True)
                    for u in range(4 * grp, 4 * grp + 4):
                        eslot = u % 8
                        gslot = u % 4
                        T.emit('sy', [f'egt{eslot}'], [f'egA{gslot}'],
                               lambda sy, eslot=eslot, gslot=gslot: sy.dma_start(
                                   out=ega[gslot][:, :],
                                   in_=egt[:, eslot * 256:eslot * 256 + 128], transpose=True),
                               dma=True)
                        T.emit('sy', [f'egt{eslot}'], [f'egB{gslot}'],
                               lambda sy, eslot=eslot, gslot=gslot: sy.dma_start(
                                   out=egb[gslot][:, :],
                                   in_=egt[:, eslot * 256 + 128:eslot * 256 + 256], transpose=True),
                               dma=True)
                    for u in range(4 * grp, 4 * grp + 4) if DBG_ATTN >= 5 else []:
                        h = HINV[u]
                        hp, b = h // 2, h % 2
                        gslot = u % 4
                        avbank = 5 + h // 8
                        avrow = 64 * (h % 2)
                        avcol = 128 * ((h // 2) % 4)
                        T.emit('t', [f'egA{gslot}', 'varena'], [f'ps{avbank}'],
                               lambda t_, hp=hp, b=b, k=k, gslot=gslot, avbank=avbank,
                                      avrow=avrow, avcol=avcol: t_.matmul(
                                   psb[avbank][avrow:avrow + 64, avcol:avcol + 128],
                                   r2(9216 + k * 1024, 9216 + k * 1024 + 1024)[:, hp * 128 + 64 * b:
                                                                               hp * 128 + 64 * b + 64],
                                   ega[gslot][:, :],
                                   start=True, stop=False, tile_position=(0, avrow)))
                        T.emit('t', [f'egB{gslot}', 'varena'], [f'ps{avbank}'],
                               lambda t_, hp=hp, b=b, k=k, gslot=gslot, avbank=avbank,
                                      avrow=avrow, avcol=avcol: t_.matmul(
                                   psb[avbank][avrow:avrow + 64, avcol:avcol + 128],
                                   r2(9216 + (k + 1) * 1024, 9216 + (k + 1) * 1024 + 1024)[:, hp * 128 + 64 * b:
                                                                                           hp * 128 + 64 * b + 64],
                                   egb[gslot][:, :],
                                   start=False, stop=True, tile_position=(0, avrow)))
                for bank in range(2 if DBG_ATTN >= 6 else 0):
                    T.emit('sc', [f'ps{5 + bank}'], ['aot'],
                           lambda s, bank=bank: s.activation(
                               out=aot[:, bank * 512:bank * 512 + 512],
                               in_=psb[5 + bank][:, :], func=AF.Copy))
                if DBG_ATTN < 6:
                    continue
                for ich in range(8):
                    for hf in range(2):
                        T.emit('t', ['aot', 'wpost'], [f'ps{7 - hf}'],
                               lambda t_, ich=ich, hf=hf: t_.matmul(
                                   psb[7 - hf][:, :],
                                   aot[:, ich * 128:ich * 128 + 128],
                                   r1(24576 + ich * 1024 + hf * 512,
                                      24576 + ich * 1024 + hf * 512 + 512),
                                   start=(ich == 0), stop=(ich == 7)))
                mxk = f'mix{k % 2}'
                for hf in range(2):
                    T.emit('v', [f'ps{7 - hf}', f'partial{k}'], [mxk],
                           lambda v, hf=hf, k=k: v.tensor_tensor(
                               mx[k % 2][:, hf * 512:hf * 512 + 512],
                               psb[7 - hf][:, :],
                               partial[:, k * 1024 + hf * 512:k * 1024 + hf * 512 + 512],
                               ALU.add))
                T.emit('g', [mxk], [f'out_{st}_{k}'],
                       lambda g, k=k, t0=t0: g.dma_start(
                           out=y_d[t0 + 128 * k:t0 + 128 * k + 128, :],
                           in_=mx[k % 2][:, :]),
                       dma=True)

        if DBG_PARTS == 'all' and DBG_ATTN < 6:
            for st in range(DBG_NST if DBG_NST is not None else NST):
                t0 = st * STT
                for k in range(NTILE):
                    T.emit('g', [f'partial{k}'], [f'outx_{st}_{k}'],
                           lambda g, k=k, t0=t0: g.dma_start(
                               out=y_d[t0 + 128 * k:t0 + 128 * k + 128, :],
                               in_=partial[:, k * 1024:k * 1024 + 1024]),
                           dma=True)
        final_g = T.cnt['g']
        T.finalize()

        @blk.tensor
        def _(t):
            T.replay('t', t, sems)

        @blk.vector
        def _(v):
            T.replay('v', v, sems)

        @blk.scalar
        def _(s):
            T.replay('sc', s, sems)

        @blk.sync
        def _(sy):
            T.replay('sy', sy, sems)

        @blk.gpsimd
        def _(g):
            T.replay('g', g, sems)
            g.wait_ge(sems['g'], final_g)

    return nc


def kernel(**inputs):
    x = np.ascontiguousarray(inputs["L0_attn_hidden_states"], dtype=np.float32)
    pos = np.ascontiguousarray(inputs["L0_attn_position_embeddings"].T)
    wqT = np.ascontiguousarray(inputs["L0_attn_w_q"].T)
    wkT = np.ascontiguousarray(inputs["L0_attn_w_k"].T)
    wvT = np.ascontiguousarray(inputs["L0_attn_w_v"].T)
    wrelT = np.ascontiguousarray(inputs["L0_attn_w_rel"].T)
    wpostT = np.ascontiguousarray(inputs["L0_attn_w_post"].T)
    combT = np.ascontiguousarray(inputs["L0_ffn_combine_w"].T)
    gwT = np.ascontiguousarray(inputs["L0_ffn_gate_w"].T)
    pds = np.ascontiguousarray(np.asarray(inputs["L0_attn_per_dim_scale"]).reshape(DH, 1))
    expT = np.ascontiguousarray(np.asarray(inputs["L0_ffn_expert_w"]).transpose(0, 2, 1))
    expb = np.ascontiguousarray(inputs["L0_ffn_expert_b"])

    nc = build_program()
    shared = dict(wqT=wqT, wkT=wkT, wvT=wvT, wrelT=wrelT, wpostT=wpostT,
                  combT=combT, gwT=gwT, pds=pds, posT=pos, expT=expT, expb=expb)
    in_maps = [dict(shared, x=np.ascontiguousarray(x[c])) for c in range(8)]
    res = run_bass_kernel_spmd(nc, in_maps, core_ids=list(range(8)))
    out = np.stack([res.results[c]["y"] for c in range(8)], axis=0)
    return out


# version 02 (final)
# speedup vs baseline: 1.0100x; 1.0100x over previous
import math
import numpy as np
from contextlib import ExitStack

import concourse.bass as bass
import concourse.mybir as mybir
from concourse.ap import AP
from concourse.bass_utils import run_bass_kernel_spmd

F32 = mybir.dt.float32
BF16 = mybir.dt.bfloat16
AF = mybir.ActivationFunctionType
ALU = mybir.AluOpType
AXX = mybir.AxisListType.X

S = 8192
HID = 1024
H = 16
DH = 64
SOFTCAP = 50.0
Q_SCALE = DH ** (-0.5) / math.log(2)
K_SCALE = math.log(1.0 + math.e) / math.log(2)
EPS = 1e-5

NST = 8
DBG_NST = None
DBG_PARTS = 'all'
DBG_ATTN = 6
STT = 1024
NTILE = 8
NLN = 10
KV_LEN = 1152
HPN = 8


def ULIN(h):
    return (2 * (h // 8) + (h % 2)) * 4 + (h // 2) % 4


HINV = [None] * 16
for _h in range(16):
    HINV[ULIN(_h)] = _h


class Trk:

    DMAQ = ('g', 'sy', 'scd')
    OWNER = {'g': 'g', 'sy': 'sy', 'scd': 'sc'}

    def __init__(self, engines):
        self.engines = engines
        self.semnames = list(engines) + ['scd']
        self.cnt = {e: 0 for e in self.semnames}
        self.waited = {e: {s: 0 for s in self.semnames} for e in engines}
        self.idx = {e: 0 for e in engines}
        self.last_drain = {e: -1 for e in engines}
        self.q = {e: [] for e in engines}
        self.b = {}
        self.alias = {}
        self.bounds = {s: [0] for s in self.DMAQ}

    def _ent(self, key):
        if key not in self.b:
            self.b[key] = {'w': {}, 'r': {}, 'wi': {}}
        return self.b[key]

    def add_alias(self, *keys):
        ks = set(keys)
        for k in keys:
            self.alias.setdefault(k, set()).update(ks - {k})

    def _need(self, ename, mydma, deps):
        for sem, val in deps.items():
            if sem == ename and not mydma and sem not in self.DMAQ:
                continue
            if sem in self.DMAQ:
                val = self.cnt[sem]
            if val == 0:
                continue
            if self.waited[ename][sem] < val:
                self.q[ename].append(('wait', sem, val))
                self.waited[ename][sem] = val
                for ds, own in self.OWNER.items():
                    if own == ename:
                        self.bounds[ds].append(self.cnt[ds])

    def _maybe_drain(self, ename, keys):
        if ename not in ('v', 'sc'):
            return
        need = False
        for key in keys:
            wi = self._ent(key)['wi'].get(ename)
            if wi is not None and wi > self.last_drain[ename] and self.idx[ename] - wi < 4:
                need = True
        if need:
            self.q[ename].append(('drain',))
            self.last_drain[ename] = self.idx[ename]
            self.idx[ename] += 1

    def emit(self, ename, ins, outs, fn, dma=False):
        sem = ename
        if ename == 'sc' and dma:
            sem = 'scd'
        inc = 16 if dma else 1
        for k in ins:
            self._need(ename, dma, self._ent(k)['w'])
        self._maybe_drain(ename, ins)
        for k in outs:
            for kk in {k} | self.alias.get(k, set()):
                ent = self._ent(kk)
                self._need(ename, dma, ent['w'])
                self._need(ename, dma, ent['r'])
        self.q[ename].append(('op', fn, inc, sem))
        self.cnt[sem] += inc
        val = self.cnt[sem]
        if not dma:
            self.waited[ename][sem] = val
        myidx = self.idx[ename]
        self.idx[ename] += 1
        for k in ins:
            self._ent(k)['r'][sem] = val
        for k in outs:
            ent = self._ent(k)
            ent['w'] = {sem: val}
            ent['r'] = {}
            ent['wi'] = {ename: myidx} if not dma else {}

    def finalize(self):
        import bisect
        bnds = {s: sorted(set(self.bounds[s] + [self.cnt[s]])) for s in self.DMAQ}
        for ename in self.engines:
            qq = self.q[ename]
            for i, item in enumerate(qq):
                if item[0] == 'wait' and item[1] in self.DMAQ:
                    bl = bnds[item[1]]
                    j = bisect.bisect_left(bl, item[2])
                    qq[i] = ('wait', item[1], bl[j])

    def replay(self, ename, eng, sems):
        for item in self.q[ename]:
            if item[0] == 'wait':
                eng.wait_ge(sems[item[1]], item[2])
            elif item[0] == 'drain':
                eng.drain()
            else:
                item[1](eng).then_inc(sems[item[3]], item[2])


def build_program():
    nc = bass.Bass("TRN2", target_bir_lowering=False)

    x_d = nc.dram_tensor("x", [S, HID], F32, kind="ExternalInput").ap()
    wqT_d = nc.dram_tensor("wqT", [HID, HID], F32, kind="ExternalInput").ap()
    wkT_d = nc.dram_tensor("wkT", [HID, HID], F32, kind="ExternalInput").ap()
    wvT_d = nc.dram_tensor("wvT", [HID, HID], F32, kind="ExternalInput").ap()
    wrelT_d = nc.dram_tensor("wrelT", [HID, HID], F32, kind="ExternalInput").ap()
    wpostT_d = nc.dram_tensor("wpostT", [HID, HID], F32, kind="ExternalInput").ap()
    combT_d = nc.dram_tensor("combT", [HID, HID], F32, kind="ExternalInput").ap()
    gwT_d = nc.dram_tensor("gwT", [HID, 8], F32, kind="ExternalInput").ap()
    pds_d = nc.dram_tensor("pds", [DH, 1], F32, kind="ExternalInput").ap()
    posT_d = nc.dram_tensor("posT", [HID, 71], F32, kind="ExternalInput").ap()
    expT_d = nc.dram_tensor("expT", [8, HID, HID], F32, kind="ExternalInput").ap()
    expb_d = nc.dram_tensor("expb", [8, HID], F32, kind="ExternalInput").ap()
    y_d = nc.dram_tensor("y", [S, HID], F32, kind="ExternalOutput").ap()
    vscr_d = nc.dram_tensor("vscr", [2 * H * 128 * 104], BF16, kind="Internal").ap()

    with ExitStack() as ctx:
        SB = lambda nm, p, f, dt=F32: ctx.enter_context(nc.sbuf_tensor(nm, [p, f], dt))

        reg1 = SB("reg1", 128, 32768, BF16)
        reg2 = SB("reg2", 128, 26624, BF16)
        xn = SB("xn", 128, 8 * 1280, BF16)
        partial = SB("partial", 128, 8 * 1024, BF16)
        scb = SB("scb", 128, H * 72, BF16)
        tt = SB("tt", 128, H * 72)
        ee = SB("ee", 128, H * 72)
        d2s = SB("d2s", 128, H * 104, BF16)
        bdt = SB("bdt", 128, H * 72, BF16)
        een = SB("een", 128, H * 192 + 96, BF16)
        egt = SB("egt", 128, 8 * 256, BF16)
        ega = [SB(f"ega{i}", 128, 128, BF16) for i in range(4)]
        egb = [SB(f"egb{i}", 128, 128, BF16) for i in range(4)]
        aot = SB("aot", 128, 8 * 128, BF16)
        ssum = SB("ssum", 128, H)
        rssum = SB("rssum", 128, H)
        xf = [SB(f"xf{i}", 128, 1024) for i in range(2)]
        stg = [SB(f"stg{i}", 128, 1024, BF16) for i in range(2)]
        mx = [SB(f"mx{i}", 128, 1024) for i in range(2)]
        glog = SB("glog", 128, 8)
        w8 = SB("w8", 128, 64)
        gA = SB("gA", 128, 1)
        gB = SB("gB", 128, 1)
        gC = SB("gC", 128, 8)
        gD = SB("gD", 128, 8)
        qsA = SB("qsA", 128, 1)
        epst = SB("epst", 128, 1)
        onesb = SB("onesb", 1, 128, BF16)
        rrv = SB("rrv", 128, HPN * 104, BF16)
        lns = SB("lns", 128, 8)
        pdsb = SB("pdsb", DH, 1)
        gwa = SB("gwa", 128, 64, BF16)

        psb = [ctx.enter_context(nc.psum_tensor(f"psb{i}", [128, 512], F32)) for i in range(8)]

        sems = {}
        for snm in ('g', 'sy', 't', 'v', 'sc', 'scd'):
            sems[snm] = ctx.enter_context(nc.semaphore("sem_" + snm))
        blk = ctx.enter_context(nc.Block())

        T = Trk(['g', 'sy', 't', 'v', 'sc'])

        REG1KEYS = ('r1slot0', 'r1slot1', 'wq', 'wk', 'wv', 'wpost', 'startup', 'posb', 'relk')
        T.add_alias(*REG1KEYS)
        REG2KEYS = ('karena', 'varena', 'qarena', 'combA', 'seA0', 'seA1', 'mixbf', 'brow')
        T.add_alias(*REG2KEYS)

        def r1(a, b):
            return reg1[:, a:b]

        def wsrc(dram_t, base_el=0):
            return AP(dram_t.tensor, base_el, [[1024, 128], [128 * 1024, 8], [1, 1024]])

        def wdst(reg, off):
            fs = reg.shape[1]
            return AP(reg[:, :].tensor, off, [[fs, 128], [1024, 8], [1, 1024]])

        def r2(a, b):
            return reg2[:, a:b]

        T.emit('g', [], ['pdsb'], lambda g: g.dma_start(out=pdsb[:, :], in_=pds_d[:, :]), dma=True)
        T.emit('sc', ['pdsb'], ['qs_e'],
               lambda s: s.activation(out=qsA[0:DH, :], in_=pdsb[:, :], func=AF.Exp))
        T.emit('v', ['qs_e'], ['qs_e1'],
               lambda v: v.tensor_scalar(qsA[0:DH, :], qsA[0:DH, :], 1.0, None, ALU.add))
        T.emit('sc', ['qs_e1'], ['qs_ln'],
               lambda s: s.activation(out=qsA[0:DH, :], in_=qsA[0:DH, :], func=AF.Ln))
        T.emit('v', ['qs_ln'], ['qs_tmp'],
               lambda v: v.tensor_scalar(qsA[0:DH, :], qsA[0:DH, :], float(Q_SCALE), None, ALU.mult))
        T.emit('sy', ['qs_tmp'], ['qsA'],
               lambda sy: sy.dma_start(out=qsA[DH:128, :], in_=qsA[0:DH, :]), dma=True)
        T.emit('v', [], ['epst'], lambda v: v.memset(epst[:, :], EPS))
        T.emit('v', [], ['onesb'], lambda v: v.memset(onesb[:, :], 1.0))
        for ich in range(8):
            T.emit('g', [], ['gwa'],
                   lambda g, ich=ich: g.dma_start(out=gwa[:, ich * 8:ich * 8 + 8],
                                                  in_=gwT_d[ich * 128:(ich + 1) * 128, :]),
                   dma=True)
        for t_, nm in ((een, 'een_z'), (egt, 'egt_z'), (scb, 'scb_z'), (tt, 'tt_z'),
                       (ee, 'ee_z'), (bdt, 'bdt_z'), (d2s, 'd2s_z'), (aot, 'aot_z')):
            T.emit('v', [], [nm], lambda v, t_=t_: v.memset(t_[:, :], 0.0))
        T.add_alias('een_z', *[f'een_{h}' for h in range(H)])
        T.add_alias('egt_z', 'mixfm', *[f'egt{i}' for i in range(8)])
        T.add_alias('scb_z', 'scb')
        T.add_alias('tt_z', 'tt')
        T.add_alias('ee_z', *[f'ee_{h}' for h in range(H)])
        T.add_alias('bdt_z', 'bdt')
        T.add_alias('d2s_z', *[f'd2s_{h}' for h in range(H)])
        T.add_alias('aot_z', 'aot')
        for j in range(NLN):
            T.add_alias('xn_zz', f'xn{j}')

        T.emit('g', [], ['startup'],
               lambda g: g.dma_start(out=wdst(reg1, 0), in_=wsrc(wrelT_d)), dma=True)
        for ich in range(8):
            T.emit('g', [], ['posb'],
                   lambda g, ich=ich: g.dma_start(out=r1(8192 + ich * 71, 8192 + ich * 71 + 71),
                                                  in_=posT_d[ich * 128:(ich + 1) * 128, :]),
                   dma=True)
        for och in range(8):
            pk = f'ps{och % 2}'
            for ich in range(8):
                T.emit('t', ['startup', 'posb'], [pk],
                       lambda t_, och=och, ich=ich: t_.matmul(
                           psb[och % 2][:, 0:71],
                           r1(ich * 1024 + och * 128, ich * 1024 + och * 128 + 128),
                           r1(8192 + ich * 71, 8192 + ich * 71 + 71),
                           start=(ich == 0), stop=(ich == 7)))
            T.emit('sc', [pk], ['relk'],
                   lambda s, och=och: s.activation(out=r1(16384 + och * 71, 16384 + och * 71 + 71),
                                                   in_=psb[och % 2][:, 0:71], func=AF.Copy))
        T.emit('v', [], ['rrv'], lambda v: v.memset(rrv[:, :], 0.0))
        for h in range(H):
            hp, b = h // 2, h % 2
            src0 = 16384 + hp * 71
            lo, hi = 64 * b, 64 * b + 64
            T.emit('v', ['relk', 'rrv'], ['rrv'],
                   lambda v, hp=hp, lo=lo, hi=hi, src0=src0:
                   v.tensor_copy(rrv[lo:hi, hp * 104:hp * 104 + 30],
                                 r1(src0 + 41, src0 + 71)[lo:hi, :]))
            T.emit('v', ['relk', 'rrv'], ['rrv'],
                   lambda v, hp=hp, lo=lo, hi=hi, src0=src0:
                   v.tensor_copy(rrv[lo:hi, hp * 104 + 31:hp * 104 + 102],
                                 r1(src0, src0 + 71)[lo:hi, :]))

        for st in range(DBG_NST if DBG_NST is not None else NST):
            t0 = st * STT
            kv0 = t0 - 31

            for j in range(NLN):
                ts = t0 + 128 * (j - 1)
                xnkey = f'xn{j}'
                if ts < 0 or ts >= S:
                    for ich in range(8):
                        T.emit('v', [], [xnkey],
                               lambda v, j=j, ich=ich: v.memset(
                                   xn[:, ich * 1280 + 128 * j: ich * 1280 + 128 * j + 128], 0.0))
                    continue
                xk = f'xf{j % 2}'
                T.emit('g', [], [xk],
                       lambda g, ts=ts, j=j: g.dma_start(out=xf[j % 2][:, :],
                                                         in_=x_d[ts:ts + 128, :]), dma=True)
                T.emit('v', [xk], ['lns_sx'],
                       lambda v, j=j: v.tensor_reduce(lns[:, 0:1], xf[j % 2][:, :], AXX, ALU.add))
                T.emit('v', [xk, 'ps6'], ['ps6', 'lns_sq0'],
                       lambda v, j=j: v.scalar_tensor_tensor(psb[6][:, :], xf[j % 2][:, 0:512], 1.0,
                                                             xf[j % 2][:, 0:512], ALU.mult, ALU.mult,
                                                             accum_out=lns[:, 1:2]))
                T.emit('v', [xk, 'ps6'], ['ps6', 'lns_sq1'],
                       lambda v, j=j: v.scalar_tensor_tensor(psb[6][:, :], xf[j % 2][:, 512:1024], 1.0,
                                                             xf[j % 2][:, 512:1024], ALU.mult, ALU.mult,
                                                             accum_out=lns[:, 7:8]))
                T.emit('v', ['lns_sq0', 'lns_sq1'], ['lns_sq'],
                       lambda v: v.tensor_add(lns[:, 1:2], lns[:, 1:2], lns[:, 7:8]))
                T.emit('v', ['lns_sx'], ['lns_negm'],
                       lambda v: v.tensor_scalar(lns[:, 2:3], lns[:, 0:1], -1.0 / HID, None, ALU.mult))
                T.emit('v', ['lns_negm'], ['lns_msq'],
                       lambda v: v.tensor_mul(lns[:, 3:4], lns[:, 2:3], lns[:, 2:3]))
                T.emit('v', ['lns_sq', 'lns_msq'], ['lns_var'],
                       lambda v: v.scalar_tensor_tensor(lns[:, 4:5], lns[:, 1:2], 1.0 / HID,
                                                        lns[:, 3:4], ALU.mult, ALU.subtract))
                T.emit('sc', ['lns_var', 'epst'], ['lns_sd'],
                       lambda s: s.activation(out=lns[:, 5:6], in_=lns[:, 4:5], func=AF.Sqrt,
                                              bias=epst[:, :]))
                T.emit('v', ['lns_sd'], ['lns_rsd'],
                       lambda v: v.reciprocal(lns[:, 6:7], lns[:, 5:6]))
                sk = f'stg{j % 2}'
                T.emit('v', [xk, 'lns_negm', 'lns_rsd'], [sk],
                       lambda v, j=j: v.tensor_scalar(stg[j % 2][:, :], xf[j % 2][:, :],
                                                      lns[:, 2:3], lns[:, 6:7], ALU.add, ALU.mult))
                for ich in range(8):
                    T.emit('sy', [sk], [xnkey],
                           lambda sy, j=j, ich=ich: sy.dma_start(
                               out=xn[:, ich * 1280 + 128 * j: ich * 1280 + 128 * j + 128],
                               in_=stg[j % 2][:, ich * 128:ich * 128 + 128], transpose=True),
                           dma=True)

            xnkeys = [f'xn{j}' for j in range(NLN)]
            core_xn = [f'xn{j}' for j in range(1, 9)]

            if DBG_PARTS == 'ln':
                for k in range(NTILE):
                    xk2 = f'xf{k % 2}'
                    T.emit('g', [], [xk2],
                           lambda g, k=k, t0=t0: g.dma_start(
                               out=xf[k % 2][:, :], in_=x_d[t0 + 128 * k:t0 + 128 * k + 128, :]),
                           dma=True)
                    T.emit('g', [xk2], [f'out_{st}_{k}'],
                           lambda g, k=k, t0=t0: g.dma_start(
                               out=y_d[t0 + 128 * k:t0 + 128 * k + 128, :],
                               in_=xf[k % 2][:, :]),
                           dma=True)
                continue
            T.emit('g', [], ['combA'],
                   lambda g: g.dma_start(out=wdst(reg2, 0), in_=wsrc(combT_d)), dma=True)
            T.emit('g', [], ['brow'],
                   lambda g: g.dma_start(out=reg2[0:1, 11264:11264 + 8192],
                                         in_=AP(expb_d.tensor, 0, [[8192, 1], [1, 8192]])),
                   dma=True)

            for k in range(NTILE):
                xnk = f'xn{k + 1}'
                for ich in range(8):
                    T.emit('t', [xnk, 'gwa'], ['ps7'],
                           lambda t_, k=k, ich=ich: t_.matmul(
                               psb[7][:, 0:8],
                               xn[:, ich * 1280 + 128 * (k + 1): ich * 1280 + 128 * (k + 1) + 128],
                               gwa[:, ich * 8:ich * 8 + 8],
                               start=(ich == 0), stop=(ich == 7)))
                T.emit('sc', ['ps7'], ['glog'],
                       lambda s, k=k: s.activation(out=glog[:, :], in_=psb[7][:, 0:8], func=AF.Exp))
                T.emit('v', ['glog'], ['gA'],
                       lambda v: v.tensor_reduce(gA[:, :], glog[:, :], AXX, ALU.add))
                T.emit('v', ['gA'], ['gB'], lambda v: v.reciprocal(gB[:, :], gA[:, :]))
                T.emit('v', ['glog', 'gB'], ['gC'],
                       lambda v: v.tensor_scalar(gC[:, :], glog[:, :], gB[:, :], None, ALU.mult))
                T.emit('v', ['gC'], ['gA2'],
                       lambda v: v.tensor_reduce(gA[:, :], gC[:, :], AXX, ALU.max))
                T.emit('v', ['gC', 'gA2'], ['gD'],
                       lambda v: v.tensor_scalar(gD[:, :], gC[:, :], gA[:, :], None, ALU.is_ge))
                T.emit('v', ['gD', 'gC'], ['gD2'],
                       lambda v: v.scalar_tensor_tensor(gD[:, :], gD[:, :], -1e9, gC[:, :],
                                                        ALU.mult, ALU.add))
                T.emit('v', ['gD2'], ['gB2'],
                       lambda v: v.tensor_reduce(gB[:, :], gD[:, :], AXX, ALU.max))
                T.emit('v', ['gC', 'gB2'], ['gD3'],
                       lambda v: v.tensor_scalar(gD[:, :], gC[:, :], gB[:, :], None, ALU.is_ge))
                T.emit('v', ['gC', 'gD3'], [f'w8_{k}'],
                       lambda v, k=k: v.tensor_mul(w8[:, 8 * k:8 * k + 8], gC[:, :], gD[:, :]))

            for gidx in range(4):
                slot = gidx % 2
                sk = f'r1slot{slot}'
                base = slot * 16384
                for e2 in range(2):
                    e = 2 * gidx + e2
                    T.emit('g', [], [sk],
                           lambda g, e=e, base=base, e2=e2: g.dma_start(
                               out=wdst(reg1, base + e2 * 8192),
                               in_=wsrc(expT_d, e * HID * HID)),
                           dma=True)
                for k in range(NTILE):
                    xnk = f'xn{k + 1}'
                    for e2 in range(2):
                        e = 2 * gidx + e2
                        for hf in range(2):
                            T.emit('t', ['brow', 'onesb'], [f'ps{2 * e2 + hf}'],
                                   lambda t_, e=e, e2=e2, hf=hf: t_.matmul(
                                       psb[2 * e2 + hf][:, :],
                                       onesb[:, :],
                                       reg2[0:1, 11264 + e * 1024 + hf * 512:
                                            11264 + e * 1024 + hf * 512 + 512],
                                       start=True, stop=False))
                    for ich in range(8):
                        for e2 in range(2):
                            for hf in range(2):
                                T.emit('t', [xnk, sk], [f'ps{2 * e2 + hf}'],
                                       lambda t_, k=k, ich=ich, e2=e2, hf=hf, base=base: t_.matmul(
                                           psb[2 * e2 + hf][:, :],
                                           xn[:, ich * 1280 + 128 * (k + 1):
                                              ich * 1280 + 128 * (k + 1) + 128],
                                           r1(base + e2 * 8192 + ich * 1024 + hf * 512,
                                              base + e2 * 8192 + ich * 1024 + hf * 512 + 512),
                                           start=False, stop=(ich == 7)))
                    for e2 in range(2):
                        e = 2 * gidx + e2
                        sek = f'seA{e2}'
                        for hf in range(2):
                            T.emit('sc', [f'ps{2 * e2 + hf}'], [sek],
                                   lambda s, e2=e2, hf=hf: s.activation(
                                       out=r2(8192 + e2 * 1024 + hf * 512,
                                              8192 + e2 * 1024 + hf * 512 + 512),
                                       in_=psb[2 * e2 + hf][:, :], func=AF.Silu))
                        if e == 0:
                            T.emit('v', [sek, f'w8_{k}'], [f'partial{k}'],
                                   lambda v, e2=e2, k=k: v.tensor_scalar(
                                       partial[:, k * 1024:k * 1024 + 1024],
                                       r2(8192 + e2 * 1024, 8192 + e2 * 1024 + 1024),
                                       w8[:, 8 * k:8 * k + 1], None, ALU.mult))
                        else:
                            T.emit('v', [sek, f'w8_{k}', f'partial{k}'], [f'partial{k}'],
                                   lambda v, e=e, e2=e2, k=k: v.scalar_tensor_tensor(
                                       partial[:, k * 1024:k * 1024 + 1024],
                                       r2(8192 + e2 * 1024, 8192 + e2 * 1024 + 1024),
                                       w8[:, 8 * k + e:8 * k + e + 1],
                                       partial[:, k * 1024:k * 1024 + 1024],
                                       ALU.mult, ALU.add))
                    if gidx == 3:
                        for ich in range(8):
                            T.emit('sy', [f'partial{k}'], ['mixfm'],
                                   lambda sy, ich=ich, k=k: sy.dma_start(
                                       out=egt[:, ich * 128:ich * 128 + 128],
                                       in_=partial[:, k * 1024 + ich * 128:
                                                   k * 1024 + ich * 128 + 128],
                                       transpose=True),
                                   dma=True)
                        for ich in range(8):
                            for hf in range(2):
                                T.emit('t', ['mixfm', 'combA'], [f'ps{4 + hf}'],
                                       lambda t_, ich=ich, hf=hf: t_.matmul(
                                           psb[4 + hf][:, :],
                                           egt[:, ich * 128:ich * 128 + 128],
                                           r2(ich * 1024 + hf * 512, ich * 1024 + hf * 512 + 512),
                                           start=(ich == 0), stop=(ich == 7)))
                        xk = f'xf{k % 2}'
                        T.emit('g', [], [xk],
                               lambda g, k=k, t0=t0: g.dma_start(
                                   out=xf[k % 2][:, :],
                                   in_=x_d[t0 + 128 * k:t0 + 128 * k + 128, :]),
                               dma=True)
                        for hf in range(2):
                            T.emit('v', [f'ps{4 + hf}', xk, f'partial{k}'], [f'partial{k}'],
                                   lambda v, k=k, hf=hf: v.tensor_tensor(
                                       partial[:, k * 1024 + hf * 512:k * 1024 + hf * 512 + 512],
                                       psb[4 + hf][:, :],
                                       xf[k % 2][:, hf * 512:hf * 512 + 512], ALU.add))
            T.emit('v', [], ['egt_rz'], lambda v: v.memset(egt[:, :], 0.0))
            T.add_alias('egt_rz', 'mixfm', *[f'egt{i}' for i in range(8)])

            if DBG_PARTS == 'ffn':
                for k in range(NTILE):
                    T.emit('g', [f'partial{k}'], [f'out_{st}_{k}'],
                           lambda g, k=k, t0=t0: g.dma_start(
                               out=y_d[t0 + 128 * k:t0 + 128 * k + 128, :],
                               in_=partial[:, k * 1024:k * 1024 + 1024]),
                           dma=True)
                continue
            for nm, dram in (('wq', wqT_d), ('wk', wkT_d), ('wv', wvT_d), ('wpost', wpostT_d)):
                off = {'wq': 0, 'wk': 8192, 'wv': 16384, 'wpost': 24576}[nm]
                T.emit('g', [], [nm],
                       lambda g, dram=dram, off=off: g.dma_start(
                           out=wdst(reg1, off), in_=wsrc(dram)), dma=True)

            for och in range(8):
                for ich in range(8):
                    for bb in range(2):
                        T.emit('t', ['wq'] + (core_xn if ich == 0 else []),
                               [f'ps{(och % 2) * 2 + bb}'],
                               lambda t_, och=och, ich=ich, bb=bb: t_.matmul(
                                   psb[(och % 2) * 2 + bb][:, :],
                                   r1(ich * 1024 + och * 128, ich * 1024 + och * 128 + 128),
                                   xn[:, ich * 1280 + 128 + bb * 512:
                                      ich * 1280 + 128 + bb * 512 + 512],
                                   start=(ich == 0), stop=(ich == 7)))
                for bb in range(2):
                    T.emit('sc', [f'ps{(och % 2) * 2 + bb}', 'qsA', 'qs_tmp'], ['qarena'],
                           lambda s, och=och, bb=bb: s.activation(
                               out=r2(18432 + och * 1024 + bb * 512,
                                      18432 + och * 1024 + bb * 512 + 512),
                               in_=psb[(och % 2) * 2 + bb][:, :],
                               func=AF.Copy, scale=qsA[:, :]))
            kblocks = [(0, 512), (512, 512), (1024, 128)]
            for och in range(8):
                for ich in range(8):
                    for bi, (bo, bn) in enumerate(kblocks):
                        T.emit('t', ['wk'] + (xnkeys if ich == 0 else []),
                               [f'ps{(och % 2) * 3 + bi}'],
                               lambda t_, och=och, ich=ich, bo=bo, bn=bn, bi=bi: t_.matmul(
                                   psb[(och % 2) * 3 + bi][:, 0:bn],
                                   r1(8192 + ich * 1024 + och * 128,
                                      8192 + ich * 1024 + och * 128 + 128),
                                   xn[:, ich * 1280 + 97 + bo: ich * 1280 + 97 + bo + bn],
                                   start=(ich == 0), stop=(ich == 7)))
                for bi, (bo, bn) in enumerate(kblocks):
                    T.emit('sc', [f'ps{(och % 2) * 3 + bi}'], ['karena'],
                           lambda s, och=och, bo=bo, bn=bn, bi=bi: s.activation(
                               out=r2(och * 1152 + bo, och * 1152 + bo + bn),
                               in_=psb[(och % 2) * 3 + bi][:, 0:bn],
                               func=AF.Copy, scale=float(K_SCALE)))
            for j in range(NLN):
                ts = t0 + 128 * (j - 1)
                if ts + 128 <= kv0 or ts >= kv0 + KV_LEN:
                    continue
                xnk = f'xn{j}'
                for hf in range(2):
                    for ich in range(8):
                        T.emit('t', [xnk, 'wv'], [f'ps{6 + hf}'],
                               lambda t_, j=j, ich=ich, hf=hf: t_.matmul(
                                   psb[6 + hf][:, :],
                                   xn[:, ich * 1280 + 128 * j: ich * 1280 + 128 * j + 128],
                                   r1(16384 + ich * 1024 + hf * 512,
                                      16384 + ich * 1024 + hf * 512 + 512),
                                   start=(ich == 0), stop=(ich == 7)))
                    T.emit('sc', [f'ps{6 + hf}'], [f'stg{j % 2}'],
                           lambda s, j=j, hf=hf: s.activation(
                               out=stg[j % 2][:, hf * 512:hf * 512 + 512],
                               in_=psb[6 + hf][:, :], func=AF.Copy))
                p0 = ts - kv0
                lo = max(0, -p0)
                hi = min(128, KV_LEN - p0)
                pa = p0 + lo
                while lo < hi:
                    blkn = pa // 128
                    row0 = pa % 128
                    n = min(hi - lo, 128 - row0)
                    T.emit('sy', [f'stg{j % 2}'], ['varena'],
                           lambda sy, lo=lo, row0=row0, blkn=blkn, n=n, j=j: sy.dma_start(
                               out=r2(9216 + blkn * 1024, 9216 + blkn * 1024 + 1024)[row0:row0 + n, :],
                               in_=stg[j % 2][lo:lo + n, :]),
                           dma=True)
                    lo += n
                    pa += n

            for k in range(NTILE if DBG_ATTN >= 2 else 0):
                for gi, heads in enumerate(([0, 2, 4, 6], [1, 3, 5, 7],
                                            [8, 10, 12, 14], [9, 11, 13, 15])):
                    par = heads[0] % 2
                    pk = f'ps{4 + par}'
                    u0 = ULIN(heads[0])
                    for j, h in enumerate(heads):
                        hp, b = h // 2, h % 2
                        qcol = 18432 + hp * 1024 + 128 * k
                        T.emit('t', ['qarena', 'rrv'], [pk],
                               lambda t_, b=b, qcol=qcol, hp=hp, par=par, j=j: t_.matmul(
                                   psb[4 + par][:, 104 * j:104 * j + 104],
                                   r2(qcol, qcol + 128)[64 * b:64 * b + 64, :],
                                   rrv[64 * b:64 * b + 64, hp * 104:hp * 104 + 104],
                                   start=True, stop=True, tile_position=(64 * b, 0)))
                    T.emit('sc', [pk], [f'd2s_{h}' for h in heads] + [f'd2g{gi}'],
                           lambda s, par=par, u0=u0: s.activation(
                               out=d2s[:, u0 * 104:u0 * 104 + 416],
                               in_=psb[4 + par][:, 0:416], func=AF.Copy))
                ring = k % 2
                vbase = ring * H * 128 * 104
                d2keys = [f'd2s_{h}' for h in range(H)]
                T.emit('sc', d2keys, [f'vscr{ring}'],
                       lambda s, vbase=vbase: s.dma_start(
                           out=AP(vscr_d.tensor, vbase + 30, [[104, 128], [13312, H], [1, 74]]),
                           in_=AP(d2s[:, :].tensor, 30, [[H * 104, 128], [104, H], [1, 74]])),
                       dma=True)
                T.emit('sc', d2keys, [f'vscr{ring}'],
                       lambda s, vbase=vbase: s.dma_start(
                           out=AP(vscr_d.tensor, vbase + 104, [[104, 127], [13312, H], [1, 30]]),
                           in_=AP(d2s[:, :].tensor, 0, [[H * 104, 127], [104, H], [1, 30]])),
                       dma=True)
                for m in range(4):
                    T.emit('sc', [f'vscr{ring}'], ['bdt'],
                           lambda s, m=m, vbase=vbase: s.dma_start(
                               out=AP(bdt[:, :].tensor, 32 * m * (H * 72),
                                      [[H * 72, 32], [72, H], [1, 71]]),
                               in_=AP(vscr_d.tensor, vbase + m * 32 * 104 + 31,
                                      [[103, 32], [13312, H], [1, 71]])),
                           dma=True)
                if DBG_ATTN < 3:
                    continue
                for h in range(H):
                    hp, b = h // 2, h % 2
                    u = ULIN(h)
                    bank = u // 4
                    slot = u % 4
                    for m in range(4):
                        qcol = 18432 + hp * 1024 + 128 * k + 32 * m
                        kcol = hp * 1152 + 128 * k + 32 * m
                        T.emit('t', ['qarena', 'karena'], [f'ps{bank}'],
                               lambda t_, b=b, m=m, qcol=qcol, kcol=kcol, bank=bank, slot=slot:
                               t_.matmul(
                                   psb[bank][32 * m:32 * m + 32, slot * 72:slot * 72 + 72],
                                   r2(qcol, qcol + 32)[64 * b:64 * b + 64, :],
                                   r2(kcol, kcol + 72)[64 * b:64 * b + 64, :],
                                   start=True, stop=True, tile_position=(64 * b, 32 * m)))
                for bank in range(4 if DBG_ATTN >= 3.2 else 0):
                    T.emit('v', [f'ps{bank}', 'bdt'], ['scb'],
                           lambda v, bank=bank: v.tensor_tensor(
                               scb[:, bank * 288:bank * 288 + 288],
                               psb[bank][:, 0:288],
                               bdt[:, bank * 288:bank * 288 + 288], ALU.add))
                if DBG_ATTN >= 3.4:
                    T.emit('sc', ['scb'], ['tt'],
                           lambda s: s.activation(out=tt[:, :], in_=scb[:, :], func=AF.Tanh,
                                                  scale=1.0 / SOFTCAP))
                for h in range(H if DBG_ATTN >= 3.6 else 0):
                    T.emit('sc', ['tt'], [f'ee_{h}', f'ssum_{h}'],
                           lambda s, h=h: s.activation(out=ee[:, h * 72:h * 72 + 71],
                                                       in_=tt[:, h * 72:h * 72 + 71],
                                                       func=AF.Exp, scale=float(SOFTCAP),
                                                       accum_out=ssum[:, h:h + 1]))
                if DBG_ATTN >= 3.8:
                    T.emit('v', [f'ssum_{u}' for u in range(H)], ['rssum'],
                           lambda v: v.reciprocal(rssum[:, :], ssum[:, :]))
                for h in range(H if DBG_ATTN >= 3.8 else 0):
                    T.emit('v', [f'ee_{h}', 'rssum'], [f'een_{h}'],
                           lambda v, h=h: v.tensor_scalar(
                               een[:, 96 + 192 * h:96 + 192 * h + 71],
                               ee[:, h * 72:h * 72 + 71],
                               rssum[:, h:h + 1], None, ALU.mult))
                if DBG_ATTN < 4:
                    continue
                for grp in range(4):
                    for u in range(4 * grp, 4 * grp + 4):
                        eslot = u % 8
                        for m in range(4):
                            T.emit('sy', [f'een_{u}'], [f'egt{eslot}'],
                                   lambda sy, u=u, m=m, eslot=eslot: sy.dma_start(
                                       out=egt[32 * m:32 * m + 32, eslot * 256:eslot * 256 + 192],
                                       in_=een[32 * m:32 * m + 32,
                                               96 + 192 * u - 32 * m:96 + 192 * u - 32 * m + 192]),
                                   dma=True)
                    for u in range(4 * grp, 4 * grp + 4):
                        eslot = u % 8
                        gslot = u % 4
                        T.emit('sy', [f'egt{eslot}'], [f'egA{gslot}'],
                               lambda sy, eslot=eslot, gslot=gslot: sy.dma_start(
                                   out=ega[gslot][:, :],
                                   in_=egt[:, eslot * 256:eslot * 256 + 128], transpose=True),
                               dma=True)
                        T.emit('sy', [f'egt{eslot}'], [f'egB{gslot}'],
                               lambda sy, eslot=eslot, gslot=gslot: sy.dma_start(
                                   out=egb[gslot][:, :],
                                   in_=egt[:, eslot * 256 + 128:eslot * 256 + 256], transpose=True),
                               dma=True)
                    for u in range(4 * grp, 4 * grp + 4) if DBG_ATTN >= 5 else []:
                        h = HINV[u]
                        hp, b = h // 2, h % 2
                        gslot = u % 4
                        avbank = 5 + h // 8
                        avrow = 64 * (h % 2)
                        avcol = 128 * ((h // 2) % 4)
                        T.emit('t', [f'egA{gslot}', 'varena'], [f'ps{avbank}'],
                               lambda t_, hp=hp, b=b, k=k, gslot=gslot, avbank=avbank,
                                      avrow=avrow, avcol=avcol: t_.matmul(
                                   psb[avbank][avrow:avrow + 64, avcol:avcol + 128],
                                   r2(9216 + k * 1024, 9216 + k * 1024 + 1024)[:, hp * 128 + 64 * b:
                                                                               hp * 128 + 64 * b + 64],
                                   ega[gslot][:, :],
                                   start=True, stop=False, tile_position=(0, avrow)))
                        T.emit('t', [f'egB{gslot}', 'varena'], [f'ps{avbank}'],
                               lambda t_, hp=hp, b=b, k=k, gslot=gslot, avbank=avbank,
                                      avrow=avrow, avcol=avcol: t_.matmul(
                                   psb[avbank][avrow:avrow + 64, avcol:avcol + 128],
                                   r2(9216 + (k + 1) * 1024, 9216 + (k + 1) * 1024 + 1024)[:, hp * 128 + 64 * b:
                                                                                           hp * 128 + 64 * b + 64],
                                   egb[gslot][:, :],
                                   start=False, stop=True, tile_position=(0, avrow)))
                for bank in range(2 if DBG_ATTN >= 6 else 0):
                    T.emit('sc', [f'ps{5 + bank}'], ['aot'],
                           lambda s, bank=bank: s.activation(
                               out=aot[:, bank * 512:bank * 512 + 512],
                               in_=psb[5 + bank][:, :], func=AF.Copy))
                if DBG_ATTN < 6:
                    continue
                for ich in range(8):
                    for hf in range(2):
                        T.emit('t', ['aot', 'wpost'], [f'ps{7 - hf}'],
                               lambda t_, ich=ich, hf=hf: t_.matmul(
                                   psb[7 - hf][:, :],
                                   aot[:, ich * 128:ich * 128 + 128],
                                   r1(24576 + ich * 1024 + hf * 512,
                                      24576 + ich * 1024 + hf * 512 + 512),
                                   start=(ich == 0), stop=(ich == 7)))
                mxk = f'mix{k % 2}'
                for hf in range(2):
                    T.emit('v', [f'ps{7 - hf}', f'partial{k}'], [mxk],
                           lambda v, hf=hf, k=k: v.tensor_tensor(
                               mx[k % 2][:, hf * 512:hf * 512 + 512],
                               psb[7 - hf][:, :],
                               partial[:, k * 1024 + hf * 512:k * 1024 + hf * 512 + 512],
                               ALU.add))
                T.emit('g', [mxk], [f'out_{st}_{k}'],
                       lambda g, k=k, t0=t0: g.dma_start(
                           out=y_d[t0 + 128 * k:t0 + 128 * k + 128, :],
                           in_=mx[k % 2][:, :]),
                       dma=True)

        if DBG_PARTS == 'all' and DBG_ATTN < 6:
            for st in range(DBG_NST if DBG_NST is not None else NST):
                t0 = st * STT
                for k in range(NTILE):
                    T.emit('g', [f'partial{k}'], [f'outx_{st}_{k}'],
                           lambda g, k=k, t0=t0: g.dma_start(
                               out=y_d[t0 + 128 * k:t0 + 128 * k + 128, :],
                               in_=partial[:, k * 1024:k * 1024 + 1024]),
                           dma=True)
        final_g = T.cnt['g']
        T.finalize()

        @blk.tensor
        def _(t):
            T.replay('t', t, sems)

        @blk.vector
        def _(v):
            T.replay('v', v, sems)

        @blk.scalar
        def _(s):
            T.replay('sc', s, sems)

        @blk.sync
        def _(sy):
            T.replay('sy', sy, sems)

        @blk.gpsimd
        def _(g):
            T.replay('g', g, sems)
            g.wait_ge(sems['g'], final_g)

    return nc


def kernel(**inputs):
    x = np.ascontiguousarray(inputs["L0_attn_hidden_states"], dtype=np.float32)
    pos = np.ascontiguousarray(inputs["L0_attn_position_embeddings"].T)
    wqT = np.ascontiguousarray(inputs["L0_attn_w_q"].T)
    wkT = np.ascontiguousarray(inputs["L0_attn_w_k"].T)
    wvT = np.ascontiguousarray(inputs["L0_attn_w_v"].T)
    wrelT = np.ascontiguousarray(inputs["L0_attn_w_rel"].T)
    wpostT = np.ascontiguousarray(inputs["L0_attn_w_post"].T)
    combT = np.ascontiguousarray(inputs["L0_ffn_combine_w"].T)
    gwT = np.ascontiguousarray(inputs["L0_ffn_gate_w"].T)
    pds = np.ascontiguousarray(np.asarray(inputs["L0_attn_per_dim_scale"]).reshape(DH, 1))
    expT = np.ascontiguousarray(np.asarray(inputs["L0_ffn_expert_w"]).transpose(0, 2, 1))
    expb = np.ascontiguousarray(inputs["L0_ffn_expert_b"])

    nc = build_program()
    shared = dict(wqT=wqT, wkT=wkT, wvT=wvT, wrelT=wrelT, wpostT=wpostT,
                  combT=combT, gwT=gwT, pds=pds, posT=pos, expT=expT, expb=expb)
    in_maps = [dict(shared, x=np.ascontiguousarray(x[c])) for c in range(8)]
    res = run_bass_kernel_spmd(nc, in_maps, core_ids=list(range(8)))
    out = np.stack([res.results[c]["y"] for c in range(8)], axis=0)
    return out
